# version 16
# baseline (speedup 1.0000x reference)
import numpy as np
import ml_dtypes
import concourse.bass as bass
import concourse.mybir as mybir
from concourse.bass_utils import run_bass_kernel_spmd

BFNP = ml_dtypes.bfloat16
F32 = mybir.dt.float32
BF16 = mybir.dt.bfloat16
AF = mybir.ActivationFunctionType
ALU = mybir.AluOpType
AX = mybir.AxisListType

S_LEN = 2048
DM = 2048
NH = 16
HD = 128
INTER = 5632
BPC = 2
TOK = BPC * S_LEN
NW = 4
WTOK = TOK // NW
SCALE = float(HD) ** -0.5
EXPB = -12.0
NEG = -1e30
NIC4 = INTER // 512


def _ovl(a0, a1, b0, b1):
    return a0 < b1 and b0 < a1


class Sync:

    def __init__(self):
        self.ch = {}
        self.wr = {}
        self.rd = {}

    def chan(self, name, stream, sem):
        self.ch[name] = dict(stream=stream, sem=sem, count=0, known={},
                             hist={})

    def op(self, cn, emit, reads=(), writes=(), dma=False):
        ch = self.ch[cn]
        need = {}

        def _need(dn, dc):
            if need.get(dn, 0) < dc:
                need[dn] = dc

        if dma and ch['count'] > 0:
            _need(cn, ch['count'])
        for (sp, lo, hi) in reads:
            for (l, h, dn, dc) in self.wr.get(sp, ()):
                if _ovl(lo, hi, l, h):
                    _need(dn, dc)
        for (sp, lo, hi) in writes:
            for (l, h, dn, dc) in self.wr.get(sp, ()):
                if _ovl(lo, hi, l, h):
                    _need(dn, dc)
            for (l, h, dn, dc) in self.rd.get(sp, ()):
                if _ovl(lo, hi, l, h):
                    _need(dn, dc)
        for dn in sorted(need):
            dc = need[dn]
            if ch['known'].get(dn, 0) >= dc:
                continue
            dsem = self.ch[dn]['sem']
            ch['stream'].append(lambda e, dsem=dsem, dc=dc:
                                e.wait_ge(dsem, dc))
            ch['known'][dn] = dc
            snap = self.ch[dn]['hist'].get(dc)
            if snap:
                for k2, v2 in snap.items():
                    if ch['known'].get(k2, 0) < v2:
                        ch['known'][k2] = v2
        step = 16 if dma else 1
        sem = ch['sem']
        ch['stream'].append(lambda e, emit=emit, sem=sem, step=step:
                            emit(e).then_inc(sem, step))
        ch['count'] += step
        ch['hist'][ch['count']] = dict(ch['known'])
        cnt = ch['count']
        for (sp, lo, hi) in writes:
            wl = self.wr.setdefault(sp, [])
            wl[:] = [e for e in wl if not (lo <= e[0] and e[1] <= hi)]
            wl.append([lo, hi, cn, cnt])
            rl = self.rd.setdefault(sp, [])
            rl[:] = [e for e in rl if not (lo <= e[0] and e[1] <= hi)]
        for (sp, lo, hi) in reads:
            rl = self.rd.setdefault(sp, [])
            for e in rl:
                if e[0] == lo and e[1] == hi and e[2] == cn:
                    e[3] = cnt
                    break
            else:
                rl.append([lo, hi, cn, cnt])

    def finals(self):
        return {n: c['count'] for n, c in self.ch.items()}



def band(qb):
    t0 = max(0, 128 * qb - 128)
    t1 = min(S_LEN, 128 * qb + 256)
    corners = []
    if qb >= 1:
        corners.append((64, 128, 0, 64))
    if qb <= 14:
        w = t1 - t0
        corners.append((0, 64, w - 64, w))
    return t0, t1, corners



XNT = 0
VSB = 32768
WVS = 40960
QTS = (49152, 51200, 53248, 55296)
KTS = (57344, 59392, 61440, 63488)
WQS = 65536
WKS = 67584
ATT = (69632, 70144, 70656, 71168)
ATTT = (71680, 72192)
OST = (72704, 74752)
XNB = (76800, 76800)
TRBA = 78848
TRBB = 81664
EB = (84480, 84992, 85504, 86016)
AOT = 32768
WOS = 0
HDN = 0
XNW = 45056
GWS = (61440, 65536)
UWS = (69632, 73728)
DWS = 45056
ARBF_N = 86528

XT = (0, 0)
SS = (2048, 4096)
DSO = 6144
SIL = (0, 512)
SSL = (1024, 1536)
XSL = (2048, 2560)
OSL = (3072, 3584)
ARF32_N = 8192


def build(dump=()):
    nc = bass.Bass("TRN2", target_bir_lowering=False, debug=False)

    def dram(name, shape, dt, kind):
        if kind == "Internal" and name in dump:
            kind = "ExternalOutput"
        return nc.dram_tensor(name, shape, dt, kind=kind).ap()

    x = dram("x", [TOK, DM], F32, "ExternalInput")
    wq = dram("wq", [DM, DM], F32, "ExternalInput")
    wk = dram("wk", [DM, DM], F32, "ExternalInput")
    wv = dram("wv", [DM, DM], F32, "ExternalInput")
    wo = dram("wo", [DM, DM], F32, "ExternalInput")
    gw = dram("gw", [INTER, DM], F32, "ExternalInput")
    uw = dram("uw", [INTER, DM], F32, "ExternalInput")
    dwn = dram("dwn", [DM, INTER], F32, "ExternalInput")
    gs = dram("gs", [128, 44], F32, "ExternalInput")
    us = dram("us", [128, 44], F32, "ExternalInput")
    ds = dram("ds", [128, 2048], F32, "ExternalInput")
    ident = dram("ident", [128, 128], BF16, "ExternalInput")
    out = dram("out", [TOK, DM], F32, "ExternalOutput")

    xn_h = dram("xn_h", [TOK, DM], BF16, "Internal")
    s_h = dram("s_h", [TOK, DM], F32, "Internal")
    ao_h = dram("ao_h", [BPC, NH, 128, S_LEN], BF16, "Internal")
    wqb = dram("wqb", [DM, DM], BF16, "Internal")
    wkb = dram("wkb", [DM, DM], BF16, "Internal")
    wvb = dram("wvb", [DM, DM], BF16, "Internal")
    wob = dram("wob", [DM, DM], BF16, "Internal")
    gwb = dram("gwb", [INTER, DM], BF16, "Internal")
    uwb = dram("uwb", [INTER, DM], BF16, "Internal")
    dwb = dram("dwb", [DM, INTER], BF16, "Internal")
    gwt2 = dram("gwt2", [16, 128, INTER], BF16, "Internal")
    uwt2 = dram("uwt2", [16, 128, INTER], BF16, "Internal")
    dwt2 = dram("dwt2", [44, 128, DM], BF16, "Internal")

    from contextlib import ExitStack
    with ExitStack() as _es:
        ec = _es.enter_context
        arbf = ec(nc.sbuf_tensor("arbf", [128, ARBF_N], BF16))
        arf32 = ec(nc.sbuf_tensor("arf32", [128, ARF32_N], F32))
        ident_sb = ec(nc.sbuf_tensor("ident_sb", [128, 128], BF16))
        eps_t = ec(nc.sbuf_tensor("eps_t", [128, 1], F32))
        b12_t = ec(nc.sbuf_tensor("b12_t", [128, 1], F32))
        negt = ec(nc.sbuf_tensor("negt", [128, 64], F32))
        bn6 = ec(nc.sbuf_tensor("bn6", [128, 24], F32))
        mv = ec(nc.sbuf_tensor("mv", [128, 2], F32))
        lnv = ec(nc.sbuf_tensor("lnv", [128, 1], F32))
        rstd = ec(nc.sbuf_tensor("rstd", [128, 1], F32))
        sums = ec(nc.sbuf_tensor("sums", [128, 4], F32))
        inv = ec(nc.sbuf_tensor("inv", [128, 4], F32))
        gs_sb = ec(nc.sbuf_tensor("gs_sb", [128, 44], F32))
        us_sb = ec(nc.sbuf_tensor("us_sb", [128, 44], F32))
        POOL = [(ec(nc.psum_tensor(f"pp{i}", [128, 512], F32)), f'pp{i}')
                for i in range(6)]
        PT = [(ec(nc.psum_tensor(f"pt{i}", [128, 384], BF16)), f'pt{i}')
              for i in range(2)]
        smT = ec(nc.semaphore("smT"))
        smV = ec(nc.semaphore("smV"))
        smS = ec(nc.semaphore("smS"))
        NQ = 6
        smQS = [ec(nc.semaphore(f"smQS{i}")) for i in range(NQ)]
        smQG = [ec(nc.semaphore(f"smQG{i}")) for i in range(NQ)]
        blk = ec(nc.Block())

        S = Sync()
        qrot = {'QS': 0, 'QG': 0}
        sems = {'T': smT, 'V': smV, 'S': smS}
        for i in range(NQ):
            sems[f'QS{i}'] = smQS[i]
            sems[f'QG{i}'] = smQG[i]
        streams = {'T': [], 'V': [], 'S': [], 'SYNC': [], 'GP': []}

        qrot_bg = {'QS': 0, 'QG': 0}

        def em(cn, emit, reads=(), writes=(), dma=False, lane=None):
            if cn in qrot:
                if lane == 'bg':
                    sub = 4 + qrot_bg[cn]
                    qrot_bg[cn] = (qrot_bg[cn] + 1) % 2
                else:
                    sub = qrot[cn]
                    qrot[cn] = (sub + 1) % 4
                ckey = f'{cn}{sub}'
                stream = streams['SYNC' if cn == 'QS' else 'GP']
            else:
                ckey = cn
                stream = streams[cn]
            if ckey not in S.ch:
                S.chan(ckey, stream, sems[ckey])
            S.op(ckey, emit, reads, writes, dma)

        prot_i = [0]

        def next_bank():
            b = POOL[prot_i[0] % 6]
            prot_i[0] += 1
            return b

        def bf(lo, n):
            return arbf[:, lo:lo + n]

        def f32(lo, n):
            return arf32[:, lo:lo + n]

        RB = 'ARBF'
        RF = 'ARF32'

        em('QS', lambda e: e.dma_start(out=ident_sb[:], in_=ident),
           writes=[('ident', 0, 1)], dma=True)
        em('QS', lambda e: e.dma_start(out=gs_sb[:], in_=gs),
           writes=[('gs', 0, 1)], dma=True)
        em('QS', lambda e: e.dma_start(out=us_sb[:], in_=us),
           writes=[('us', 0, 1)], dma=True)
        em('V', lambda e: e.memset(eps_t[:], 1e-5), writes=[('eps', 0, 1)])
        em('V', lambda e: e.memset(b12_t[:], EXPB), writes=[('b12', 0, 1)])
        em('V', lambda e: e.memset(negt[:], NEG), writes=[('negt', 0, 1)])

        units = []

        def d2d_cast_unit(srcw, dst, nm, r0, nr):
            def u():
                em('QG', lambda e: e.dma_start(
                    out=dst[r0:r0 + nr, :], in_=srcw[r0:r0 + nr, :]),
                    writes=[(nm, r0, r0 + nr)], dma=True, lane='bg')
            return u

        for r in range(4):
            units.append([d2d_cast_unit(wo, wob, 'wob', 512 * r, 512)])
        for r in range(8):
            units.append([d2d_cast_unit(gw, gwb, 'gwb', 704 * r, 704)])
        for r in range(8):
            units.append([d2d_cast_unit(uw, uwb, 'uwb', 704 * r, 704)])
        for r in range(8):
            units.append([d2d_cast_unit(dwn, dwb, 'dwb', 256 * r, 256)])

        def trans_unit(srcb, dstb, nms, nmd, col, r0, w, rr1, wkey, buf):
            def s1():
                em('QS', lambda e: e.dma_start(
                    out=bf(buf, w), in_=srcb[r0:r0 + w,
                                             128 * col:128 * (col + 1)],
                    transpose=True),
                    reads=[(nms, 0, rr1)], writes=[(RB, buf, buf + w)],
                    dma=True, lane='bg')

            def s2():
                em('QG', lambda e: e.dma_start(
                    out=dstb[col][:, r0:r0 + w], in_=bf(buf, w)),
                    reads=[(RB, buf, buf + w)],
                    writes=[(nmd, wkey, wkey + 1)],
                    dma=True, lane='bg')
            return [s1, s2]

        ti = 0
        for hc in range(16):
            for hh in range(2):
                units.append(trans_unit(gwb, gwt2, 'gwb', 'gwt2', hc,
                                        2816 * hh, 2816, 5632, 2 * hc + hh,
                                        (TRBA, TRBB)[ti % 2]))
                ti += 1
        for hc in range(16):
            for hh in range(2):
                units.append(trans_unit(uwb, uwt2, 'uwb', 'uwt2', hc,
                                        2816 * hh, 2816, 5632, 2 * hc + hh,
                                        (TRBA, TRBB)[ti % 2]))
                ti += 1
        for ic in range(44):
            units.append(trans_unit(dwb, dwt2, 'dwb', 'dwt2', ic, 0, 2048,
                                    2048, ic, (TRBA, TRBB)[ti % 2]))
            ti += 1

        slots = {}
        for t, u in enumerate(units):
            for k, m in enumerate(u):
                slots.setdefault(t + k, []).append(m)
        slot_list = [slots[t] for t in sorted(slots)]

        def p1_tile(g, slot):
            xt = XT[slot]
            xb = XNB[slot]
            em('QS', lambda e: e.dma_start(
                out=f32(xt, 2048), in_=x[128 * g:128 * (g + 1), :]),
                writes=[(RF, xt, xt + 2048)], dma=True, lane='bg')
            for c in range(4):
                em('V', lambda e, c=c: e.bn_stats(
                    bn6[:, 6 * c:6 * (c + 1)], f32(xt + 512 * c, 512)),
                    reads=[(RF, xt, xt + 2048)], writes=[('bn6', c, c + 1)])
            em('V', lambda e: e.bn_aggr(mv[:], bn6[:]),
               reads=[('bn6', 0, 4)], writes=[('mv', 0, 2)])
            em('S', lambda e: e.activation(
                lnv[:], mv[:, 1:2], AF.Ln, bias=eps_t[:, 0:1]),
                reads=[('mv', 0, 2), ('eps', 0, 1)], writes=[('lnv', 0, 1)])
            em('S', lambda e: e.activation(rstd[:], lnv[:], AF.Exp,
                                           scale=-0.5),
                reads=[('lnv', 0, 1)], writes=[('rstd', 0, 1)])
            em('V', lambda e: e.tensor_scalar(
                bf(xb, 2048), f32(xt, 2048), mv[:, 0:1], rstd[:, 0:1],
                op0=ALU.subtract, op1=ALU.mult),
                reads=[(RF, xt, xt + 2048), ('mv', 0, 2), ('rstd', 0, 1)],
                writes=[(RB, xb, xb + 2048)])
            em('QG', lambda e: e.dma_start(
                out=xn_h[128 * g:128 * (g + 1), :], in_=bf(xb, 2048)),
                reads=[(RB, xb, xb + 2048)],
                writes=[('xn_h', g, g + 1)], dma=True, lane='bg')

        def wslice_load(head, wsrc, wnm, wsl):
            ops = []
            for k in range(4):
                def u(e, k=k, head=head, wsrc=wsrc, wsl=wsl):
                    return e.dma_start(
                        out=bf(wsl + 512 * k, 512).rearrange(
                            "p (c d) -> p c d", d=128),
                        in_=wsrc[512 * k:512 * (k + 1),
                                 128 * head:128 * (head + 1)].rearrange(
                            "(c p) d -> p c d", p=128))
                ops.append(lambda u=u, k=k, wsl=wsl, wnm=wnm: em(
                    'QS', u, reads=[(wnm, 512 * k, 512 * (k + 1))],
                    writes=[(RB, wsl + 512 * k,
                             wsl + 512 * (k + 1))], dma=True))
            return ops

        def proj_ops(head, seq):
            par = head % 4
            ops = []
            ops += wslice_load(head, wqb, 'wqb', WQS)
            ops += wslice_load(head, wkb, 'wkb', WKS)
            for (wsl, dstt) in ((WQS, QTS), (WKS, KTS)):
                dst = dstt[par]
                for tc in range(4):
                    def grp(tc=tc, wsl=wsl, dst=dst):
                        pb, pk = next_bank()

                        def mmg(e, tc=tc, wsl=wsl, pb=pb):
                            for hc in range(16):
                                i = e.matmul(
                                    pb[:], bf(wsl + 128 * hc, 128),
                                    bf(XNT + 2048 * hc + 512 * tc, 512),
                                    start=(hc == 0), stop=(hc == 15))
                            return i
                        em('T', mmg,
                           reads=[(RB, wsl, wsl + 2048)] +
                                 [(RB, XNT + 2048 * hc + 512 * tc,
                                   XNT + 2048 * hc + 512 * (tc + 1))
                                  for hc in range(16)],
                           writes=[(pk, 0, 1)])
                        em('V', lambda e, dst=dst, tc=tc, pb=pb: e.tensor_copy(
                            bf(dst + 512 * tc, 512), pb[:]),
                            reads=[(pk, 0, 1)],
                            writes=[(RB, dst + 512 * tc,
                                     dst + 512 * (tc + 1))])
                    ops.append(grp)
            return ops

        def vproj_ops(grp_i):
            ops = []
            for tt in range(16):
                def g(tt=tt):
                    pb, pk = next_bank()

                    def mmg(e, tt=tt, pb=pb):
                        for hc in range(16):
                            i = e.matmul(
                                pb[:], bf(XNT + 2048 * hc + 128 * tt, 128),
                                bf(WVS + 512 * hc, 512),
                                start=(hc == 0), stop=(hc == 15))
                        return i
                    em('T', mmg,
                       reads=[(RB, WVS, WVS + 8192)] +
                             [(RB, XNT + 2048 * hc + 128 * tt,
                               XNT + 2048 * hc + 128 * (tt + 1))
                              for hc in range(16)],
                       writes=[(pk, 0, 1)])
                    em('V', lambda e, tt=tt, pb=pb: e.tensor_copy(
                        bf(VSB + 512 * tt, 512), pb[:]),
                        reads=[(pk, 0, 1)],
                        writes=[(RB, VSB + 512 * tt, VSB + 512 * (tt + 1))])
                ops.append(g)
            return ops

        def wv_load_ops(grp_i):
            ops = []
            for k in range(4):
                def u(e, k=k, grp_i=grp_i):
                    return e.dma_start(
                        out=bf(WVS + 2048 * k, 2048).rearrange(
                            "p (c d) -> p c d", d=512),
                        in_=wvb[512 * k:512 * (k + 1),
                                512 * grp_i:512 * (grp_i + 1)].rearrange(
                            "(c p) d -> p c d", p=128))
                ops.append(lambda u=u, k=k: em(
                    'QS', u, reads=[('wvb', 512 * k, 512 * (k + 1))],
                    writes=[(RB, WVS + 2048 * k,
                             WVS + 2048 * (k + 1))], dma=True))
            return ops

        def wo_load_ops(seq):
            ops = []
            for k in range(8):
                def u(e, k=k):
                    return e.dma_start(
                        out=bf(WOS + 4096 * k, 4096).rearrange(
                            "p (c h) -> p c h", h=2048),
                        in_=wob[256 * k:256 * (k + 1), :].rearrange(
                            "(c p) h -> p c h", p=128))
                ops.append(lambda u=u, k=k: em(
                    'QS', u, reads=[('wob', 256 * k, 256 * (k + 1))],
                    writes=[(RB, WOS + 4096 * k,
                             WOS + 4096 * (k + 1))], dma=True))
            return ops

        def p2_seq(seq, second):
            for half in range(2):
                for hc in range(16):
                    em('QS', lambda e, hc=hc, half=half: e.dma_start(
                        out=bf(XNT + 2048 * hc + 1024 * half, 1024),
                        in_=xn_h[S_LEN * seq + 1024 * half:
                                 S_LEN * seq + 1024 * (half + 1),
                                 128 * hc:128 * (hc + 1)], transpose=True),
                        reads=[('xn_h', 16 * seq + 8 * half,
                                16 * seq + 8 * (half + 1))],
                        writes=[(RB, XNT + 2048 * hc + 1024 * half,
                                 XNT + 2048 * hc + 1024 * (half + 1))],
                        dma=True)
            for o in wv_load_ops(0):
                o()
            for o in proj_ops(0, seq):
                o()
            for o in proj_ops(1, seq):
                o()

            for hp in range(8):
                heads = (2 * hp, 2 * hp + 1)
                grp = hp // 2
                if hp % 2 == 0:
                    prim_v = vproj_ops(grp)
                else:
                    prim_v = []
                base_f = []
                if hp % 2 == 1 and hp < 7:
                    base_f += wv_load_ops(grp + 1)
                if hp < 7:
                    base_f += proj_ops(heads[0] + 2, seq)
                    base_f += proj_ops(heads[1] + 2, seq)
                else:
                    base_f += wo_load_ops(seq)
                fillers = []
                for fi_, f_ in enumerate(base_f):
                    fillers.append(f_)
                    if fi_ % 4 == 3:
                        fillers.extend(second)
                fillers.extend(second * 2)

                def stageA(head, qb):
                    par4 = head % 4
                    sl = head % 2
                    qts, kts = QTS[par4], KTS[par4]
                    t0, t1, corners = band(qb)
                    W = t1 - t0
                    pb, pk = next_bank()
                    em('T', lambda e, qb=qb, t0=t0, W=W, pb=pb, qts=qts,
                        kts=kts: e.matmul(
                            pb[:, 0:W], bf(qts + 128 * qb, 128),
                            bf(kts + t0, W), start=True, stop=True),
                        reads=[(RB, qts, qts + 2048), (RB, kts, kts + 2048)],
                        writes=[(pk, 0, 1)])
                    for (r0, r1, c0, c1) in corners:
                        em('V', lambda e, pb=pb, r0=r0, r1=r1, c0=c0, c1=c1:
                            e.memset(pb[r0:r1, c0:c1], NEG),
                            writes=[(pk, 0, 1)])
                    sc4 = 2 * sl + qb % 2
                    eb = EB[sc4]
                    em('S', lambda e, pb=pb, W=W, eb=eb, sc4=sc4:
                        e.activation(
                            bf(eb, W), pb[:, 0:W], AF.Exp, scale=SCALE,
                            bias=b12_t[:, 0:1],
                            accum_out=sums[:, sc4:sc4 + 1]),
                        reads=[(pk, 0, 1), ('b12', 0, 1)],
                        writes=[(RB, eb, eb + W), ('sums', sc4, sc4 + 1)])

                def stageB(head, qb):
                    sl = head % 2
                    hh = head % 4
                    ost = OST[sl]
                    t0, t1, _ = band(qb)
                    W = t1 - t0
                    nchunk = W // 128
                    sc4 = 2 * sl + qb % 2
                    eb = EB[sc4]
                    att = ATT[2 * sl + qb % 2]
                    attt = ATTT[sl]
                    ptb, ptk = PT[sl]
                    pa, pak = next_bank()
                    em('V', lambda e, sc4=sc4: e.reciprocal(
                        inv[:, sc4:sc4 + 1], sums[:, sc4:sc4 + 1]),
                        reads=[('sums', sc4, sc4 + 1)],
                        writes=[('inv', sc4, sc4 + 1)])
                    em('S', lambda e, W=W, eb=eb, att=att, sc4=sc4:
                        e.activation(bf(att, W), bf(eb, W), AF.Copy,
                                     scale=inv[:, sc4:sc4 + 1]),
                        reads=[(RB, eb, eb + W), ('inv', sc4, sc4 + 1)],
                        writes=[(RB, att, att + W)])

                    def tmm(e, nchunk=nchunk, att=att, ptb=ptb):
                        for c in range(nchunk):
                            i = e.transpose(ptb[:, 128 * c:128 * (c + 1)],
                                            bf(att + 128 * c, 128),
                                            ident_sb[:])
                        return i
                    em('T', tmm,
                       reads=[(RB, att, att + 384), ('ident', 0, 1)],
                       writes=[(ptk, 0, 1)])
                    em('V', lambda e, W=W, attt=attt, ptb=ptb: e.tensor_copy(
                        bf(attt, W), ptb[:, 0:W]),
                        reads=[(ptk, 0, 1)], writes=[(RB, attt, attt + W)])

                    def avmm(e, t0=t0, nchunk=nchunk, hh=hh, pa=pa,
                             attt=attt):
                        for c in range(nchunk):
                            tt = t0 // 128 + c
                            i = e.matmul(
                                pa[:, 0:128],
                                bf(VSB + 512 * tt + 128 * hh, 128),
                                bf(attt + 128 * c, 128),
                                start=(c == 0), stop=(c == nchunk - 1))
                        return i
                    em('T', avmm,
                       reads=[(RB, attt, attt + 384), (RB, VSB, VSB + 8192)],
                       writes=[(pak, 0, 1)])
                    em('V', lambda e, ost=ost, qb=qb, pa=pa: e.tensor_copy(
                        bf(ost + 128 * qb, 128), pa[:, 0:128]),
                        reads=[(pak, 0, 1)],
                        writes=[(RB, ost + 128 * qb, ost + 128 * (qb + 1))])

                fi = [0]

                def fill(n):
                    while n > 0 and fi[0] < len(fillers):
                        fillers[fi[0]]()
                        fi[0] += 1
                        n -= 1

                vp = [0]

                def vfill(n):
                    while n > 0 and vp[0] < len(prim_v):
                        prim_v[vp[0]]()
                        vp[0] += 1
                        n -= 1

                for j in range(18):
                    vfill(2)
                    if j >= 2:
                        stageB(heads[0], j - 2)
                        stageB(heads[1], j - 2)
                    if j < 16:
                        stageA(heads[0], j)
                        stageA(heads[1], j)
                    fill(3)
                vfill(16)
                fill(len(fillers))
                for sl in range(2):
                    em('QG', lambda e, seq=seq, head=heads[sl], ost=OST[sl]:
                        e.dma_start(out=ao_h[seq, head], in_=bf(ost, 2048)),
                        reads=[(RB, OST[sl], OST[sl] + 2048)],
                        writes=[('ao', 16 * seq + heads[sl],
                                 16 * seq + heads[sl] + 1)], dma=True)
            for k in range(4):
                em('QS', lambda e, seq=seq, k=k: e.dma_start(
                    out=bf(AOT + 8192 * k, 8192).rearrange(
                        "p (c t) -> p c t", t=2048),
                    in_=ao_h[seq, 4 * k:4 * (k + 1)].rearrange(
                        "c p t -> p c t")),
                    reads=[('ao', 16 * seq + 4 * k, 16 * seq + 4 * (k + 1))],
                    writes=[(RB, AOT + 8192 * k, AOT + 8192 * (k + 1))],
                    dma=True)
            for tt in range(16):
                ssl = SS[tt % 2]
                for hc in range(4):
                    pb, pk = next_bank()

                    def womm(e, tt=tt, hc=hc, pb=pb):
                        for dc in range(16):
                            i = e.matmul(
                                pb[:], bf(AOT + 2048 * dc + 128 * tt, 128),
                                bf(WOS + 2048 * dc + 512 * hc, 512),
                                start=(dc == 0), stop=(dc == 15))
                        return i
                    em('T', womm,
                       reads=[(RB, AOT, AOT + 32768), (RB, WOS, WOS + 32768)],
                       writes=[(pk, 0, 1)])
                    em('V', lambda e, ssl=ssl, hc=hc, pb=pb: e.tensor_copy(
                        f32(ssl + 512 * hc, 512), pb[:]),
                        reads=[(pk, 0, 1)],
                        writes=[(RF, ssl + 512 * hc, ssl + 512 * (hc + 1))])
                g = seq * 16 + tt
                em('QG', lambda e, g=g, ssl=ssl: e.dma_start(
                    out=s_h[128 * g:128 * (g + 1), :], in_=f32(ssl, 2048)),
                    reads=[(RF, ssl, ssl + 2048)],
                    writes=[('s_h', g, g + 1)], dma=True, lane='bg')

        def p3_window(w):
            tok0 = WTOK * w
            for hc in range(16):
                em('QS', lambda e, hc=hc, tok0=tok0: e.dma_start(
                    out=bf(XNW + 1024 * hc, 1024),
                    in_=xn_h[tok0:tok0 + WTOK, 128 * hc:128 * (hc + 1)],
                    transpose=True),
                    reads=[('xn_h', tok0 // 128, tok0 // 128 + 8)],
                    writes=[(RB, XNW + 1024 * hc, XNW + 1024 * (hc + 1))],
                    dma=True)
            for ic2 in range(22):
                par = ic2 % 2
                for (cache, dstt, nm) in ((gwt2, GWS, 'gwt2'),
                                          (uwt2, UWS, 'uwt2')):
                    dst = dstt[par]
                    for k in range(4):
                        em('QS', lambda e, cache=cache, dst=dst, ic2=ic2, k=k:
                            e.dma_start(
                                out=bf(dst + 1024 * k, 1024).rearrange(
                                    "p (c i) -> p c i", i=256),
                                in_=cache[4 * k:4 * (k + 1), :,
                                          256 * ic2:256 * (ic2 + 1)]
                                .rearrange("c p i -> p c i")),
                            reads=[(nm, 8 * k, 8 * (k + 1))],
                            writes=[(RB, dst + 1024 * k,
                                     dst + 1024 * (k + 1))], dma=True)
                for sub in range(2):
                    ic = 2 * ic2 + sub
                    for tc in range(2):
                        pg, pgk = next_bank()
                        pu, puk = next_bank()

                        def gmm(dstoff, pb, e, sub=sub, tc=tc):
                            for hc in range(16):
                                i = e.matmul(
                                    pb[:],
                                    bf(dstoff + 256 * hc + 128 * sub, 128),
                                    bf(XNW + 1024 * hc + 512 * tc, 512),
                                    start=(hc == 0), stop=(hc == 15))
                            return i
                        em('T', lambda e, gmm=gmm, par=par, pg=pg: gmm(
                            GWS[par], pg, e),
                            reads=[(RB, GWS[par], GWS[par] + 4096),
                                   (RB, XNW, XNW + 16384)],
                            writes=[(pgk, 0, 1)])
                        em('T', lambda e, gmm=gmm, par=par, pu=pu: gmm(
                            UWS[par], pu, e),
                            reads=[(RB, UWS[par], UWS[par] + 4096),
                                   (RB, XNW, XNW + 16384)],
                            writes=[(puk, 0, 1)])
                        sil = SIL[tc % 2]
                        em('S', lambda e, sil=sil, pg=pg, ic=ic: e.activation(
                            f32(sil, 512), pg[:], AF.Silu,
                            scale=gs_sb[:, ic:ic + 1]),
                            reads=[(pgk, 0, 1), ('gs', 0, 1)],
                            writes=[(RF, sil, sil + 512)])
                        ho = HDN + 1024 * ic + 512 * tc
                        em('V', lambda e, sil=sil, pu=pu, ho=ho, ic=ic:
                            e.scalar_tensor_tensor(
                                bf(ho, 512), pu[:], us_sb[:, ic:ic + 1],
                                f32(sil, 512), op0=ALU.mult, op1=ALU.mult),
                            reads=[(RF, sil, sil + 512), (puk, 0, 1),
                                   ('us', 0, 1)],
                            writes=[(RB, ho, ho + 512)])
            for hc in range(4):
                for k in range(11):
                    em('QS', lambda e, hc=hc, k=k: e.dma_start(
                        out=bf(DWS + 2048 * k, 2048).rearrange(
                            "p (c h) -> p c h", h=512),
                        in_=dwt2[4 * k:4 * (k + 1), :,
                                 512 * hc:512 * (hc + 1)]
                        .rearrange("c p h -> p c h")),
                        reads=[('dwt2', 4 * k, 4 * (k + 1))],
                        writes=[(RB, DWS + 2048 * k, DWS + 2048 * (k + 1))],
                        dma=True)
                for tt in range(8):
                    g = 8 * w + tt
                    r0, r1 = 128 * g, 128 * (g + 1)
                    c0 = 512 * hc
                    pb, pk = next_bank()
                    ssl = SSL[tt % 2]
                    xsl = XSL[tt % 2]
                    osl = OSL[tt % 2]
                    sil = SIL[tt % 2]
                    em('QS', lambda e, ssl=ssl, r0=r0, r1=r1, c0=c0:
                        e.dma_start(out=f32(ssl, 512),
                                    in_=s_h[r0:r1, c0:c0 + 512]),
                        reads=[('s_h', r0 // 128, r1 // 128)],
                        writes=[(RF, ssl, ssl + 512)], dma=True)
                    em('QS', lambda e, xsl=xsl, r0=r0, r1=r1, c0=c0:
                        e.dma_start(out=f32(xsl, 512),
                                    in_=x[r0:r1, c0:c0 + 512]),
                        writes=[(RF, xsl, xsl + 512)], dma=True)

                    def dmmA(e, tt=tt, pb=pb):
                        for ic in range(24):
                            i = e.matmul(
                                pb[:], bf(HDN + 1024 * ic + 128 * tt, 128),
                                bf(DWS + 512 * ic, 512),
                                start=(ic == 0), stop=False)
                        return i

                    def dmmB(e, tt=tt, pb=pb):
                        for ic in range(24, 44):
                            i = e.matmul(
                                pb[:], bf(HDN + 1024 * ic + 128 * tt, 128),
                                bf(DWS + 512 * ic, 512),
                                start=False, stop=(ic == 43))
                        return i
                    em('T', dmmA,
                       reads=[(RB, HDN, HDN + 45056),
                              (RB, DWS, DWS + 12288)],
                       writes=[(pk, 0, 1)])
                    em('T', dmmB,
                       reads=[(RB, HDN, HDN + 45056),
                              (RB, DWS + 12288, DWS + 22528)],
                       writes=[(pk, 0, 1)])
                    em('V', lambda e, osl=osl, pb=pb, c0=c0:
                        e.tensor_tensor(f32(osl, 512), pb[:],
                                        f32(DSO + c0, 512), op=ALU.mult),
                        reads=[(pk, 0, 1), (RF, DSO + c0, DSO + c0 + 512)],
                        writes=[(RF, osl, osl + 512)])
                    em('S', lambda e, sil=sil, osl=osl: e.activation(
                        f32(sil, 512), f32(osl, 512), AF.Silu),
                        reads=[(RF, osl, osl + 512)],
                        writes=[(RF, sil, sil + 512)])
                    em('V', lambda e, osl=osl, sil=sil, ssl=ssl:
                        e.tensor_tensor(f32(osl, 512), f32(sil, 512),
                                        f32(ssl, 512), op=ALU.mult),
                        reads=[(RF, sil, sil + 512), (RF, ssl, ssl + 512)],
                        writes=[(RF, osl, osl + 512)])
                    em('V', lambda e, xsl=xsl, osl=osl:
                        e.tensor_tensor(f32(xsl, 512), f32(osl, 512),
                                        f32(xsl, 512), op=ALU.add),
                        reads=[(RF, osl, osl + 512), (RF, xsl, xsl + 512)],
                        writes=[(RF, xsl, xsl + 512)])
                    em('QG', lambda e, xsl=xsl, r0=r0, r1=r1, c0=c0,
                        g=g, hc=hc: e.dma_start(
                            out=out[r0:r1, c0:c0 + 512], in_=f32(xsl, 512)),
                        reads=[(RF, xsl, xsl + 512)],
                        writes=[('out', 4 * g + hc, 4 * g + hc + 1)],
                        dma=True, lane='bg')

        for (wsrc, wdst, wnm) in ((wv, wvb, 'wvb'), (wq, wqb, 'wqb'),
                                  (wk, wkb, 'wkb')):
            for r in range(4):
                d2d_cast_unit(wsrc, wdst, wnm, 512 * r, 512)()
        for t in range(16):
            p1_tile(t, t % 2)

        second0 = []
        for t in range(16):
            second0.append(lambda t=t: p1_tile(16 + t, t % 2))

        def mkslot(ms):
            def run():
                for m in ms:
                    m()
            return run
        for ms in slot_list:
            second0.append(mkslot(ms))

        class _Queue:

            def __init__(self, items):
                self.items = items
                self.i = 0

            def __call__(self):
                if self.i < len(self.items):
                    self.items[self.i]()
                    self.i += 1

        sq = _Queue(second0)
        p2_seq(0, [sq])
        while sq.i < min(16, len(sq.items)):
            sq()
        p2_seq(1, [sq])
        while sq.i < len(sq.items):
            sq()
        em('QS', lambda e: e.dma_start(out=f32(DSO, 2048), in_=ds),
           writes=[(RF, DSO, DSO + 2048)], dma=True)
        for w in range(NW):
            p3_window(w)

        f = S.finals()
        for cn, cnt in sorted(f.items()):
            if cnt > 0:
                streams['SYNC'].append(
                    lambda e, sem=sems[cn], cnt=cnt: e.wait_ge(sem, cnt))

        def _replay(stream):
            def body(eng):
                for fn in stream:
                    fn(eng)
            return body
        blk.tensor(_replay(streams['T']))
        blk.vector(_replay(streams['V']))
        blk.scalar(_replay(streams['S']))
        blk.sync(_replay(streams['SYNC']))
        blk.gpsimd(_replay(streams['GP']))

    return nc



def kernel(L0_attn_hidden_states, L0_attn_Wq, L0_attn_Wk, L0_attn_Wv,
           L0_attn_Wo, L0_ffn_gate_w, L0_ffn_up_w, L0_ffn_down_w,
           L0_ffn_gate_s, L0_ffn_up_s, L0_ffn_down_s):
    xs = np.ascontiguousarray(np.asarray(L0_attn_hidden_states,
                                         dtype=np.float32))
    base = {
        "wq": np.ascontiguousarray(L0_attn_Wq, dtype=np.float32),
        "wk": np.ascontiguousarray(L0_attn_Wk, dtype=np.float32),
        "wv": np.ascontiguousarray(L0_attn_Wv, dtype=np.float32),
        "wo": np.ascontiguousarray(L0_attn_Wo, dtype=np.float32),
        "gw": np.ascontiguousarray(L0_ffn_gate_w, dtype=np.float32),
        "uw": np.ascontiguousarray(L0_ffn_up_w, dtype=np.float32),
        "dwn": np.ascontiguousarray(L0_ffn_down_w, dtype=np.float32),
        "gs": np.ascontiguousarray(
            np.asarray(L0_ffn_gate_s, dtype=np.float32).reshape(44, 128).T),
        "us": np.ascontiguousarray(
            np.asarray(L0_ffn_up_s, dtype=np.float32).reshape(44, 128).T),
        "ds": np.ascontiguousarray(np.broadcast_to(
            np.asarray(L0_ffn_down_s, dtype=np.float32)[None, :],
            (128, DM))),
        "ident": np.eye(128, dtype=BFNP),
    }
    nc = build()
    in_maps = []
    for c in range(8):
        m = dict(base)
        m["x"] = np.ascontiguousarray(
            xs[BPC * c:BPC * (c + 1)].reshape(TOK, DM))
        in_maps.append(m)
    res = run_bass_kernel_spmd(nc, in_maps, core_ids=list(range(8)))
    shards = [r["out"].reshape(BPC, S_LEN, DM) for r in res.results]
    return np.ascontiguousarray(np.concatenate(shards, axis=0))


# version 17
# speedup vs baseline: 1.3580x; 1.3580x over previous
import numpy as np
import ml_dtypes
import concourse.bass as bass
import concourse.mybir as mybir
from concourse.bass_utils import run_bass_kernel_spmd

BFNP = ml_dtypes.bfloat16
F32 = mybir.dt.float32
BF16 = mybir.dt.bfloat16
AF = mybir.ActivationFunctionType
ALU = mybir.AluOpType
AX = mybir.AxisListType

S_LEN = 2048
DM = 2048
NH = 16
HD = 128
INTER = 5632
BPC = 2
TOK = BPC * S_LEN
NW = 4
WTOK = TOK // NW
SCALE = float(HD) ** -0.5
EXPB = -12.0
NEG = -1e30
NIC4 = INTER // 512


def _ovl(a0, a1, b0, b1):
    return a0 < b1 and b0 < a1


class Sync:

    def __init__(self):
        self.ch = {}
        self.wr = {}
        self.rd = {}

    def chan(self, name, stream, sem):
        self.ch[name] = dict(stream=stream, sem=sem, count=0, known={},
                             hist={})

    def op(self, cn, emit, reads=(), writes=(), dma=False):
        ch = self.ch[cn]
        need = {}

        def _need(dn, dc):
            if need.get(dn, 0) < dc:
                need[dn] = dc

        if dma and ch['count'] > 0:
            _need(cn, ch['count'])
        for (sp, lo, hi) in reads:
            for (l, h, dn, dc) in self.wr.get(sp, ()):
                if _ovl(lo, hi, l, h):
                    _need(dn, dc)
        for (sp, lo, hi) in writes:
            for (l, h, dn, dc) in self.wr.get(sp, ()):
                if _ovl(lo, hi, l, h):
                    _need(dn, dc)
            for (l, h, dn, dc) in self.rd.get(sp, ()):
                if _ovl(lo, hi, l, h):
                    _need(dn, dc)
        for dn in sorted(need):
            dc = need[dn]
            if ch['known'].get(dn, 0) >= dc:
                continue
            dsem = self.ch[dn]['sem']
            ch['stream'].append(lambda e, dsem=dsem, dc=dc:
                                e.wait_ge(dsem, dc))
            ch['known'][dn] = dc
            snap = self.ch[dn]['hist'].get(dc)
            if snap:
                for k2, v2 in snap.items():
                    if ch['known'].get(k2, 0) < v2:
                        ch['known'][k2] = v2
        step = 16 if dma else 1
        sem = ch['sem']
        ch['stream'].append(lambda e, emit=emit, sem=sem, step=step:
                            emit(e).then_inc(sem, step))
        ch['count'] += step
        ch['hist'][ch['count']] = dict(ch['known'])
        cnt = ch['count']
        for (sp, lo, hi) in writes:
            wl = self.wr.setdefault(sp, [])
            wl[:] = [e for e in wl if not (lo <= e[0] and e[1] <= hi)]
            wl.append([lo, hi, cn, cnt])
            rl = self.rd.setdefault(sp, [])
            rl[:] = [e for e in rl if not (lo <= e[0] and e[1] <= hi)]
        for (sp, lo, hi) in reads:
            rl = self.rd.setdefault(sp, [])
            for e in rl:
                if e[0] == lo and e[1] == hi and e[2] == cn:
                    e[3] = cnt
                    break
            else:
                rl.append([lo, hi, cn, cnt])

    def finals(self):
        return {n: c['count'] for n, c in self.ch.items()}



def band(qb):
    t0 = max(0, 128 * qb - 128)
    t1 = min(S_LEN, 128 * qb + 256)
    corners = []
    if qb >= 1:
        corners.append((64, 128, 0, 64))
    if qb <= 14:
        w = t1 - t0
        corners.append((0, 64, w - 64, w))
    return t0, t1, corners



XNT = 0
VSB = 32768
WVS = 40960
QTS = (49152, 51200, 53248, 55296)
KTS = (57344, 59392, 61440, 63488)
WQS = 65536
WKS = 67584
ATT = (69632, 70144, 70656, 71168)
ATTT = (71680, 72192)
OST = (72704, 74752)
XNB = (76800, 76800)
TRBA = 78848
TRBB = 81664
EB = (84480, 84992, 85504, 86016)
ATTT2 = (86528, 87040, 87552, 88064)
AOT = 32768
WOS = 0
HDN = 0
XNW = 45056
GWS = (61440, 65536)
UWS = (69632, 73728)
DWS = 45056
ARBF_N = 88576

XT = (0, 0)
SS = (2048, 4096)
DSO = 6144
SIL = (0, 512)
SSL = (1024, 1536)
XSL = (2048, 2560)
OSL = (3072, 3584)
ARF32_N = 8192


def build(dump=()):
    nc = bass.Bass("TRN2", target_bir_lowering=False, debug=False)

    def dram(name, shape, dt, kind):
        if kind == "Internal" and name in dump:
            kind = "ExternalOutput"
        return nc.dram_tensor(name, shape, dt, kind=kind).ap()

    x = dram("x", [TOK, DM], F32, "ExternalInput")
    wq = dram("wq", [DM, DM], F32, "ExternalInput")
    wk = dram("wk", [DM, DM], F32, "ExternalInput")
    wv = dram("wv", [DM, DM], F32, "ExternalInput")
    wo = dram("wo", [DM, DM], F32, "ExternalInput")
    gw = dram("gw", [INTER, DM], F32, "ExternalInput")
    uw = dram("uw", [INTER, DM], F32, "ExternalInput")
    dwn = dram("dwn", [DM, INTER], F32, "ExternalInput")
    gs = dram("gs", [128, 44], F32, "ExternalInput")
    us = dram("us", [128, 44], F32, "ExternalInput")
    ds = dram("ds", [128, 2048], F32, "ExternalInput")
    ident = dram("ident", [128, 128], BF16, "ExternalInput")
    out = dram("out", [TOK, DM], F32, "ExternalOutput")

    xn_h = dram("xn_h", [TOK, DM], BF16, "Internal")
    s_h = dram("s_h", [TOK, DM], F32, "Internal")
    ao_h = dram("ao_h", [BPC, NH, 128, S_LEN], BF16, "Internal")
    wqb = dram("wqb", [DM, DM], BF16, "Internal")
    wkb = dram("wkb", [DM, DM], BF16, "Internal")
    wvb = dram("wvb", [DM, DM], BF16, "Internal")
    wob = dram("wob", [DM, DM], BF16, "Internal")
    gwb = dram("gwb", [INTER, DM], BF16, "Internal")
    uwb = dram("uwb", [INTER, DM], BF16, "Internal")
    dwb = dram("dwb", [DM, INTER], BF16, "Internal")
    gwt2 = dram("gwt2", [16, 128, INTER], BF16, "Internal")
    uwt2 = dram("uwt2", [16, 128, INTER], BF16, "Internal")
    dwt2 = dram("dwt2", [44, 128, DM], BF16, "Internal")

    from contextlib import ExitStack
    with ExitStack() as _es:
        ec = _es.enter_context
        arbf = ec(nc.sbuf_tensor("arbf", [128, ARBF_N], BF16))
        arf32 = ec(nc.sbuf_tensor("arf32", [128, ARF32_N], F32))
        ident_sb = ec(nc.sbuf_tensor("ident_sb", [128, 128], BF16))
        eps_t = ec(nc.sbuf_tensor("eps_t", [128, 1], F32))
        b12_t = ec(nc.sbuf_tensor("b12_t", [128, 1], F32))
        negt = ec(nc.sbuf_tensor("negt", [128, 64], F32))
        bn6 = ec(nc.sbuf_tensor("bn6", [128, 24], F32))
        mv = ec(nc.sbuf_tensor("mv", [128, 2], F32))
        lnv = ec(nc.sbuf_tensor("lnv", [128, 1], F32))
        rstd = ec(nc.sbuf_tensor("rstd", [128, 1], F32))
        sums = ec(nc.sbuf_tensor("sums", [128, 4], F32))
        inv = ec(nc.sbuf_tensor("inv", [128, 4], F32))
        gs_sb = ec(nc.sbuf_tensor("gs_sb", [128, 44], F32))
        us_sb = ec(nc.sbuf_tensor("us_sb", [128, 44], F32))
        POOL = [(ec(nc.psum_tensor(f"pp{i}", [128, 512], F32)), f'pp{i}')
                for i in range(6)]
        PT = [(ec(nc.psum_tensor(f"pt{i}", [128, 384], BF16)), f'pt{i}')
              for i in range(2)]
        smT = ec(nc.semaphore("smT"))
        smV = ec(nc.semaphore("smV"))
        smS = ec(nc.semaphore("smS"))
        NQ = 6
        smQS = [ec(nc.semaphore(f"smQS{i}")) for i in range(NQ)]
        smQG = [ec(nc.semaphore(f"smQG{i}")) for i in range(NQ)]
        blk = ec(nc.Block())

        S = Sync()
        qrot = {'QS': 0, 'QG': 0}
        sems = {'T': smT, 'V': smV, 'S': smS}
        for i in range(NQ):
            sems[f'QS{i}'] = smQS[i]
            sems[f'QG{i}'] = smQG[i]
        streams = {'T': [], 'V': [], 'S': [], 'SYNC': [], 'GP': []}

        qrot_bg = {'QS': 0, 'QG': 0}

        def em(cn, emit, reads=(), writes=(), dma=False, lane=None):
            if cn in qrot:
                if lane == 'bg':
                    sub = 4 + qrot_bg[cn]
                    qrot_bg[cn] = (qrot_bg[cn] + 1) % 2
                else:
                    sub = qrot[cn]
                    qrot[cn] = (sub + 1) % 4
                ckey = f'{cn}{sub}'
                stream = streams['SYNC' if cn == 'QS' else 'GP']
            else:
                ckey = cn
                stream = streams[cn]
            if ckey not in S.ch:
                S.chan(ckey, stream, sems[ckey])
            S.op(ckey, emit, reads, writes, dma)

        prot_i = [0]

        def next_bank():
            b = POOL[prot_i[0] % 6]
            prot_i[0] += 1
            return b

        def bf(lo, n):
            return arbf[:, lo:lo + n]

        def f32(lo, n):
            return arf32[:, lo:lo + n]

        RB = 'ARBF'
        RF = 'ARF32'

        em('QS', lambda e: e.dma_start(out=ident_sb[:], in_=ident),
           writes=[('ident', 0, 1)], dma=True)
        em('QS', lambda e: e.dma_start(out=gs_sb[:], in_=gs),
           writes=[('gs', 0, 1)], dma=True)
        em('QS', lambda e: e.dma_start(out=us_sb[:], in_=us),
           writes=[('us', 0, 1)], dma=True)
        em('V', lambda e: e.memset(eps_t[:], 1e-5), writes=[('eps', 0, 1)])
        em('V', lambda e: e.memset(b12_t[:], EXPB), writes=[('b12', 0, 1)])
        em('V', lambda e: e.memset(negt[:], NEG), writes=[('negt', 0, 1)])

        units = []

        def d2d_cast_unit(srcw, dst, nm, r0, nr):
            def u():
                em('QG', lambda e: e.dma_start(
                    out=dst[r0:r0 + nr, :], in_=srcw[r0:r0 + nr, :]),
                    writes=[(nm, r0, r0 + nr)], dma=True, lane='bg')
            return u

        for r in range(4):
            units.append([d2d_cast_unit(wo, wob, 'wob', 512 * r, 512)])
        for r in range(8):
            units.append([d2d_cast_unit(gw, gwb, 'gwb', 704 * r, 704)])
        for r in range(8):
            units.append([d2d_cast_unit(uw, uwb, 'uwb', 704 * r, 704)])
        for r in range(8):
            units.append([d2d_cast_unit(dwn, dwb, 'dwb', 256 * r, 256)])

        def trans_unit(srcb, dstb, nms, nmd, col, r0, w, rr1, wkey, buf):
            def s1():
                em('QS', lambda e: e.dma_start(
                    out=bf(buf, w), in_=srcb[r0:r0 + w,
                                             128 * col:128 * (col + 1)],
                    transpose=True),
                    reads=[(nms, 0, rr1)], writes=[(RB, buf, buf + w)],
                    dma=True, lane='bg')

            def s2():
                em('QG', lambda e: e.dma_start(
                    out=dstb[col][:, r0:r0 + w], in_=bf(buf, w)),
                    reads=[(RB, buf, buf + w)],
                    writes=[(nmd, wkey, wkey + 1)],
                    dma=True, lane='bg')
            return [s1, s2]

        ti = 0
        for hc in range(16):
            for hh in range(2):
                units.append(trans_unit(gwb, gwt2, 'gwb', 'gwt2', hc,
                                        2816 * hh, 2816, 5632, 2 * hc + hh,
                                        (TRBA, TRBB)[ti % 2]))
                ti += 1
        for hc in range(16):
            for hh in range(2):
                units.append(trans_unit(uwb, uwt2, 'uwb', 'uwt2', hc,
                                        2816 * hh, 2816, 5632, 2 * hc + hh,
                                        (TRBA, TRBB)[ti % 2]))
                ti += 1
        for ic in range(44):
            units.append(trans_unit(dwb, dwt2, 'dwb', 'dwt2', ic, 0, 2048,
                                    2048, ic, (TRBA, TRBB)[ti % 2]))
            ti += 1

        slots = {}
        for t, u in enumerate(units):
            for k, m in enumerate(u):
                slots.setdefault(t + k, []).append(m)
        slot_list = [slots[t] for t in sorted(slots)]

        def p1_tile(g, slot):
            xt = XT[slot]
            xb = XNB[slot]
            em('QS', lambda e: e.dma_start(
                out=f32(xt, 2048), in_=x[128 * g:128 * (g + 1), :]),
                writes=[(RF, xt, xt + 2048)], dma=True, lane='bg')
            for c in range(4):
                em('V', lambda e, c=c: e.bn_stats(
                    bn6[:, 6 * c:6 * (c + 1)], f32(xt + 512 * c, 512)),
                    reads=[(RF, xt, xt + 2048)], writes=[('bn6', c, c + 1)])
            em('V', lambda e: e.bn_aggr(mv[:], bn6[:]),
               reads=[('bn6', 0, 4)], writes=[('mv', 0, 2)])
            em('S', lambda e: e.activation(
                lnv[:], mv[:, 1:2], AF.Ln, bias=eps_t[:, 0:1]),
                reads=[('mv', 0, 2), ('eps', 0, 1)], writes=[('lnv', 0, 1)])
            em('S', lambda e: e.activation(rstd[:], lnv[:], AF.Exp,
                                           scale=-0.5),
                reads=[('lnv', 0, 1)], writes=[('rstd', 0, 1)])
            em('V', lambda e: e.tensor_scalar(
                bf(xb, 2048), f32(xt, 2048), mv[:, 0:1], rstd[:, 0:1],
                op0=ALU.subtract, op1=ALU.mult),
                reads=[(RF, xt, xt + 2048), ('mv', 0, 2), ('rstd', 0, 1)],
                writes=[(RB, xb, xb + 2048)])
            em('QG', lambda e: e.dma_start(
                out=xn_h[128 * g:128 * (g + 1), :], in_=bf(xb, 2048)),
                reads=[(RB, xb, xb + 2048)],
                writes=[('xn_h', g, g + 1)], dma=True, lane='bg')

        def wslice_load(head, wsrc, wnm, wsl):
            ops = []
            for k in range(4):
                def u(e, k=k, head=head, wsrc=wsrc, wsl=wsl):
                    return e.dma_start(
                        out=bf(wsl + 512 * k, 512).rearrange(
                            "p (c d) -> p c d", d=128),
                        in_=wsrc[512 * k:512 * (k + 1),
                                 128 * head:128 * (head + 1)].rearrange(
                            "(c p) d -> p c d", p=128))
                ops.append(lambda u=u, k=k, wsl=wsl, wnm=wnm: em(
                    'QS', u, reads=[(wnm, 512 * k, 512 * (k + 1))],
                    writes=[(RB, wsl + 512 * k,
                             wsl + 512 * (k + 1))], dma=True))
            return ops

        def proj_ops(head, seq):
            par = head % 4
            ops = []
            ops += wslice_load(head, wqb, 'wqb', WQS)
            ops += wslice_load(head, wkb, 'wkb', WKS)
            for (wsl, dstt) in ((WQS, QTS), (WKS, KTS)):
                dst = dstt[par]
                for tc in range(4):
                    def grp(tc=tc, wsl=wsl, dst=dst):
                        pb, pk = next_bank()

                        def mmg(e, tc=tc, wsl=wsl, pb=pb):
                            for hc in range(16):
                                i = e.matmul(
                                    pb[:], bf(wsl + 128 * hc, 128),
                                    bf(XNT + 2048 * hc + 512 * tc, 512),
                                    start=(hc == 0), stop=(hc == 15))
                            return i
                        em('T', mmg,
                           reads=[(RB, wsl, wsl + 2048)] +
                                 [(RB, XNT + 2048 * hc + 512 * tc,
                                   XNT + 2048 * hc + 512 * (tc + 1))
                                  for hc in range(16)],
                           writes=[(pk, 0, 1)])
                        em('V', lambda e, dst=dst, tc=tc, pb=pb: e.tensor_copy(
                            bf(dst + 512 * tc, 512), pb[:]),
                            reads=[(pk, 0, 1)],
                            writes=[(RB, dst + 512 * tc,
                                     dst + 512 * (tc + 1))])
                    ops.append(grp)
            return ops

        def vproj_ops(grp_i):
            ops = []
            for tt in range(16):
                def g(tt=tt):
                    pb, pk = next_bank()

                    def mmg(e, tt=tt, pb=pb):
                        for hc in range(16):
                            i = e.matmul(
                                pb[:], bf(XNT + 2048 * hc + 128 * tt, 128),
                                bf(WVS + 512 * hc, 512),
                                start=(hc == 0), stop=(hc == 15))
                        return i
                    em('T', mmg,
                       reads=[(RB, WVS, WVS + 8192)] +
                             [(RB, XNT + 2048 * hc + 128 * tt,
                               XNT + 2048 * hc + 128 * (tt + 1))
                              for hc in range(16)],
                       writes=[(pk, 0, 1)])
                    em('V', lambda e, tt=tt, pb=pb: e.tensor_copy(
                        bf(VSB + 512 * tt, 512), pb[:]),
                        reads=[(pk, 0, 1)],
                        writes=[(RB, VSB + 512 * tt, VSB + 512 * (tt + 1))])
                ops.append(g)
            return ops

        def wv_load_ops(grp_i):
            ops = []
            for k in range(4):
                def u(e, k=k, grp_i=grp_i):
                    return e.dma_start(
                        out=bf(WVS + 2048 * k, 2048).rearrange(
                            "p (c d) -> p c d", d=512),
                        in_=wvb[512 * k:512 * (k + 1),
                                512 * grp_i:512 * (grp_i + 1)].rearrange(
                            "(c p) d -> p c d", p=128))
                ops.append(lambda u=u, k=k: em(
                    'QS', u, reads=[('wvb', 512 * k, 512 * (k + 1))],
                    writes=[(RB, WVS + 2048 * k,
                             WVS + 2048 * (k + 1))], dma=True))
            return ops

        def wo_load_ops(seq):
            ops = []
            for k in range(8):
                def u(e, k=k):
                    return e.dma_start(
                        out=bf(WOS + 4096 * k, 4096).rearrange(
                            "p (c h) -> p c h", h=2048),
                        in_=wob[256 * k:256 * (k + 1), :].rearrange(
                            "(c p) h -> p c h", p=128))
                ops.append(lambda u=u, k=k: em(
                    'QS', u, reads=[('wob', 256 * k, 256 * (k + 1))],
                    writes=[(RB, WOS + 4096 * k,
                             WOS + 4096 * (k + 1))], dma=True))
            return ops

        def p2_seq(seq, second):
            for half in range(2):
                for hc in range(16):
                    em('QS', lambda e, hc=hc, half=half: e.dma_start(
                        out=bf(XNT + 2048 * hc + 1024 * half, 1024),
                        in_=xn_h[S_LEN * seq + 1024 * half:
                                 S_LEN * seq + 1024 * (half + 1),
                                 128 * hc:128 * (hc + 1)], transpose=True),
                        reads=[('xn_h', 16 * seq + 8 * half,
                                16 * seq + 8 * (half + 1))],
                        writes=[(RB, XNT + 2048 * hc + 1024 * half,
                                 XNT + 2048 * hc + 1024 * (half + 1))],
                        dma=True)
            for o in wv_load_ops(0):
                o()
            for o in proj_ops(0, seq):
                o()
            for o in proj_ops(1, seq):
                o()

            for hp in range(8):
                heads = (2 * hp, 2 * hp + 1)
                grp = hp // 2
                if hp % 2 == 0:
                    prim_v = vproj_ops(grp)
                else:
                    prim_v = []
                base_f = []
                if hp % 2 == 1 and hp < 7:
                    base_f += wv_load_ops(grp + 1)
                if hp < 7:
                    base_f += proj_ops(heads[0] + 2, seq)
                    base_f += proj_ops(heads[1] + 2, seq)
                else:
                    base_f += wo_load_ops(seq)
                fillers = []
                for fi_, f_ in enumerate(base_f):
                    fillers.append(f_)
                    if fi_ % 4 == 3:
                        fillers.extend(second)
                fillers.extend(second * 2)

                def stageA(head, qb):
                    par4 = head % 4
                    sl = head % 2
                    qts, kts = QTS[par4], KTS[par4]
                    t0, t1, corners = band(qb)
                    W = t1 - t0
                    pb, pk = next_bank()
                    em('T', lambda e, qb=qb, t0=t0, W=W, pb=pb, qts=qts,
                        kts=kts: e.matmul(
                            pb[:, 0:W], bf(qts + 128 * qb, 128),
                            bf(kts + t0, W), start=True, stop=True),
                        reads=[(RB, qts, qts + 2048), (RB, kts, kts + 2048)],
                        writes=[(pk, 0, 1)])
                    for (r0, r1, c0, c1) in corners:
                        em('V', lambda e, pb=pb, r0=r0, r1=r1, c0=c0, c1=c1:
                            e.memset(pb[r0:r1, c0:c1], NEG),
                            writes=[(pk, 0, 1)])
                    sc4 = 2 * sl + qb % 2
                    eb = EB[sc4]
                    em('S', lambda e, pb=pb, W=W, eb=eb, sc4=sc4:
                        e.activation(
                            bf(eb, W), pb[:, 0:W], AF.Exp, scale=SCALE,
                            bias=b12_t[:, 0:1],
                            accum_out=sums[:, sc4:sc4 + 1]),
                        reads=[(pk, 0, 1), ('b12', 0, 1)],
                        writes=[(RB, eb, eb + W), ('sums', sc4, sc4 + 1)])

                def stageM(head, qb):
                    sl = head % 2
                    sc4 = 2 * sl + qb % 2
                    eb = EB[sc4]
                    att = ATT[2 * sl + qb % 2]
                    t0, t1, _ = band(qb)
                    W = t1 - t0
                    em('V', lambda e, sc4=sc4: e.reciprocal(
                        inv[:, sc4:sc4 + 1], sums[:, sc4:sc4 + 1]),
                        reads=[('sums', sc4, sc4 + 1)],
                        writes=[('inv', sc4, sc4 + 1)])
                    em('S', lambda e, W=W, eb=eb, att=att, sc4=sc4:
                        e.activation(bf(att, W), bf(eb, W), AF.Copy,
                                     scale=inv[:, sc4:sc4 + 1]),
                        reads=[(RB, eb, eb + W), ('inv', sc4, sc4 + 1)],
                        writes=[(RB, att, att + W)])

                def stageB1(head, qb):
                    sl = head % 2
                    att = ATT[2 * sl + qb % 2]
                    attt = ATTT2[2 * sl + qb % 2]
                    ptb, ptk = PT[sl]
                    t0, t1, _ = band(qb)
                    W = t1 - t0
                    nchunk = W // 128

                    def tmm(e, nchunk=nchunk, att=att, ptb=ptb):
                        for c in range(nchunk):
                            i = e.transpose(ptb[:, 128 * c:128 * (c + 1)],
                                            bf(att + 128 * c, 128),
                                            ident_sb[:])
                        return i
                    em('T', tmm,
                       reads=[(RB, att, att + 384), ('ident', 0, 1)],
                       writes=[(ptk, 0, 1)])
                    em('V', lambda e, W=W, attt=attt, ptb=ptb: e.tensor_copy(
                        bf(attt, W), ptb[:, 0:W]),
                        reads=[(ptk, 0, 1)], writes=[(RB, attt, attt + W)])

                def stageB2(head, qb):
                    sl = head % 2
                    hh = head % 4
                    ost = OST[sl]
                    attt = ATTT2[2 * sl + qb % 2]
                    t0, t1, _ = band(qb)
                    W = t1 - t0
                    nchunk = W // 128
                    pa, pak = next_bank()

                    def avmm(e, t0=t0, nchunk=nchunk, hh=hh, pa=pa,
                             attt=attt):
                        for c in range(nchunk):
                            tt = t0 // 128 + c
                            i = e.matmul(
                                pa[:, 0:128],
                                bf(VSB + 512 * tt + 128 * hh, 128),
                                bf(attt + 128 * c, 128),
                                start=(c == 0), stop=(c == nchunk - 1))
                        return i
                    em('T', avmm,
                       reads=[(RB, attt, attt + 384), (RB, VSB, VSB + 8192)],
                       writes=[(pak, 0, 1)])
                    em('V', lambda e, ost=ost, qb=qb, pa=pa: e.tensor_copy(
                        bf(ost + 128 * qb, 128), pa[:, 0:128]),
                        reads=[(pak, 0, 1)],
                        writes=[(RB, ost + 128 * qb, ost + 128 * (qb + 1))])

                fi = [0]

                def fill(n):
                    while n > 0 and fi[0] < len(fillers):
                        fillers[fi[0]]()
                        fi[0] += 1
                        n -= 1

                vp = [0]

                def vfill(n):
                    while n > 0 and vp[0] < len(prim_v):
                        prim_v[vp[0]]()
                        vp[0] += 1
                        n -= 1

                for j in range(20):
                    vfill(2)
                    if 2 <= j < 18:
                        stageM(heads[0], j - 2)
                        stageM(heads[1], j - 2)
                    if 3 <= j < 19:
                        stageB1(heads[0], j - 3)
                        stageB1(heads[1], j - 3)
                    if j >= 4:
                        stageB2(heads[0], j - 4)
                        stageB2(heads[1], j - 4)
                    if j < 16:
                        stageA(heads[0], j)
                        stageA(heads[1], j)
                    fill(3)
                vfill(16)
                fill(len(fillers))
                for sl in range(2):
                    em('QG', lambda e, seq=seq, head=heads[sl], ost=OST[sl]:
                        e.dma_start(out=ao_h[seq, head], in_=bf(ost, 2048)),
                        reads=[(RB, OST[sl], OST[sl] + 2048)],
                        writes=[('ao', 16 * seq + heads[sl],
                                 16 * seq + heads[sl] + 1)], dma=True)
            for k in range(4):
                em('QS', lambda e, seq=seq, k=k: e.dma_start(
                    out=bf(AOT + 8192 * k, 8192).rearrange(
                        "p (c t) -> p c t", t=2048),
                    in_=ao_h[seq, 4 * k:4 * (k + 1)].rearrange(
                        "c p t -> p c t")),
                    reads=[('ao', 16 * seq + 4 * k, 16 * seq + 4 * (k + 1))],
                    writes=[(RB, AOT + 8192 * k, AOT + 8192 * (k + 1))],
                    dma=True)
            for tt in range(16):
                ssl = SS[tt % 2]
                for hc in range(4):
                    pb, pk = next_bank()

                    def womm(e, tt=tt, hc=hc, pb=pb):
                        for dc in range(16):
                            i = e.matmul(
                                pb[:], bf(AOT + 2048 * dc + 128 * tt, 128),
                                bf(WOS + 2048 * dc + 512 * hc, 512),
                                start=(dc == 0), stop=(dc == 15))
                        return i
                    em('T', womm,
                       reads=[(RB, AOT, AOT + 32768), (RB, WOS, WOS + 32768)],
                       writes=[(pk, 0, 1)])
                    em('V', lambda e, ssl=ssl, hc=hc, pb=pb: e.tensor_copy(
                        f32(ssl + 512 * hc, 512), pb[:]),
                        reads=[(pk, 0, 1)],
                        writes=[(RF, ssl + 512 * hc, ssl + 512 * (hc + 1))])
                g = seq * 16 + tt
                em('QG', lambda e, g=g, ssl=ssl: e.dma_start(
                    out=s_h[128 * g:128 * (g + 1), :], in_=f32(ssl, 2048)),
                    reads=[(RF, ssl, ssl + 2048)],
                    writes=[('s_h', g, g + 1)], dma=True, lane='bg')

        def p3_window(w):
            tok0 = WTOK * w
            for hc in range(16):
                em('QS', lambda e, hc=hc, tok0=tok0: e.dma_start(
                    out=bf(XNW + 1024 * hc, 1024),
                    in_=xn_h[tok0:tok0 + WTOK, 128 * hc:128 * (hc + 1)],
                    transpose=True),
                    reads=[('xn_h', tok0 // 128, tok0 // 128 + 8)],
                    writes=[(RB, XNW + 1024 * hc, XNW + 1024 * (hc + 1))],
                    dma=True)
            for ic2 in range(22):
                par = ic2 % 2
                for (cache, dstt, nm) in ((gwt2, GWS, 'gwt2'),
                                          (uwt2, UWS, 'uwt2')):
                    dst = dstt[par]
                    for k in range(4):
                        em('QS', lambda e, cache=cache, dst=dst, ic2=ic2, k=k:
                            e.dma_start(
                                out=bf(dst + 1024 * k, 1024).rearrange(
                                    "p (c i) -> p c i", i=256),
                                in_=cache[4 * k:4 * (k + 1), :,
                                          256 * ic2:256 * (ic2 + 1)]
                                .rearrange("c p i -> p c i")),
                            reads=[(nm, 8 * k, 8 * (k + 1))],
                            writes=[(RB, dst + 1024 * k,
                                     dst + 1024 * (k + 1))], dma=True)
                for sub in range(2):
                    ic = 2 * ic2 + sub
                    for tc in range(2):
                        pg, pgk = next_bank()
                        pu, puk = next_bank()

                        def gmm(dstoff, pb, e, sub=sub, tc=tc):
                            for hc in range(16):
                                i = e.matmul(
                                    pb[:],
                                    bf(dstoff + 256 * hc + 128 * sub, 128),
                                    bf(XNW + 1024 * hc + 512 * tc, 512),
                                    start=(hc == 0), stop=(hc == 15))
                            return i
                        em('T', lambda e, gmm=gmm, par=par, pg=pg: gmm(
                            GWS[par], pg, e),
                            reads=[(RB, GWS[par], GWS[par] + 4096),
                                   (RB, XNW, XNW + 16384)],
                            writes=[(pgk, 0, 1)])
                        em('T', lambda e, gmm=gmm, par=par, pu=pu: gmm(
                            UWS[par], pu, e),
                            reads=[(RB, UWS[par], UWS[par] + 4096),
                                   (RB, XNW, XNW + 16384)],
                            writes=[(puk, 0, 1)])
                        sil = SIL[tc % 2]
                        em('S', lambda e, sil=sil, pg=pg, ic=ic: e.activation(
                            f32(sil, 512), pg[:], AF.Silu,
                            scale=gs_sb[:, ic:ic + 1]),
                            reads=[(pgk, 0, 1), ('gs', 0, 1)],
                            writes=[(RF, sil, sil + 512)])
                        ho = HDN + 1024 * ic + 512 * tc
                        em('V', lambda e, sil=sil, pu=pu, ho=ho, ic=ic:
                            e.scalar_tensor_tensor(
                                bf(ho, 512), pu[:], us_sb[:, ic:ic + 1],
                                f32(sil, 512), op0=ALU.mult, op1=ALU.mult),
                            reads=[(RF, sil, sil + 512), (puk, 0, 1),
                                   ('us', 0, 1)],
                            writes=[(RB, ho, ho + 512)])
            for hc in range(4):
                for k in range(11):
                    em('QS', lambda e, hc=hc, k=k: e.dma_start(
                        out=bf(DWS + 2048 * k, 2048).rearrange(
                            "p (c h) -> p c h", h=512),
                        in_=dwt2[4 * k:4 * (k + 1), :,
                                 512 * hc:512 * (hc + 1)]
                        .rearrange("c p h -> p c h")),
                        reads=[('dwt2', 4 * k, 4 * (k + 1))],
                        writes=[(RB, DWS + 2048 * k, DWS + 2048 * (k + 1))],
                        dma=True)
                for tt in range(8):
                    g = 8 * w + tt
                    r0, r1 = 128 * g, 128 * (g + 1)
                    c0 = 512 * hc
                    pb, pk = next_bank()
                    ssl = SSL[tt % 2]
                    xsl = XSL[tt % 2]
                    osl = OSL[tt % 2]
                    sil = SIL[tt % 2]
                    em('QS', lambda e, ssl=ssl, r0=r0, r1=r1, c0=c0:
                        e.dma_start(out=f32(ssl, 512),
                                    in_=s_h[r0:r1, c0:c0 + 512]),
                        reads=[('s_h', r0 // 128, r1 // 128)],
                        writes=[(RF, ssl, ssl + 512)], dma=True)
                    em('QS', lambda e, xsl=xsl, r0=r0, r1=r1, c0=c0:
                        e.dma_start(out=f32(xsl, 512),
                                    in_=x[r0:r1, c0:c0 + 512]),
                        writes=[(RF, xsl, xsl + 512)], dma=True)

                    def dmmA(e, tt=tt, pb=pb):
                        for ic in range(24):
                            i = e.matmul(
                                pb[:], bf(HDN + 1024 * ic + 128 * tt, 128),
                                bf(DWS + 512 * ic, 512),
                                start=(ic == 0), stop=False)
                        return i

                    def dmmB(e, tt=tt, pb=pb):
                        for ic in range(24, 44):
                            i = e.matmul(
                                pb[:], bf(HDN + 1024 * ic + 128 * tt, 128),
                                bf(DWS + 512 * ic, 512),
                                start=False, stop=(ic == 43))
                        return i
                    em('T', dmmA,
                       reads=[(RB, HDN, HDN + 45056),
                              (RB, DWS, DWS + 12288)],
                       writes=[(pk, 0, 1)])
                    em('T', dmmB,
                       reads=[(RB, HDN, HDN + 45056),
                              (RB, DWS + 12288, DWS + 22528)],
                       writes=[(pk, 0, 1)])
                    em('V', lambda e, osl=osl, pb=pb, c0=c0:
                        e.tensor_tensor(f32(osl, 512), pb[:],
                                        f32(DSO + c0, 512), op=ALU.mult),
                        reads=[(pk, 0, 1), (RF, DSO + c0, DSO + c0 + 512)],
                        writes=[(RF, osl, osl + 512)])
                    em('S', lambda e, sil=sil, osl=osl: e.activation(
                        f32(sil, 512), f32(osl, 512), AF.Silu),
                        reads=[(RF, osl, osl + 512)],
                        writes=[(RF, sil, sil + 512)])
                    em('V', lambda e, osl=osl, sil=sil, ssl=ssl:
                        e.tensor_tensor(f32(osl, 512), f32(sil, 512),
                                        f32(ssl, 512), op=ALU.mult),
                        reads=[(RF, sil, sil + 512), (RF, ssl, ssl + 512)],
                        writes=[(RF, osl, osl + 512)])
                    em('V', lambda e, xsl=xsl, osl=osl:
                        e.tensor_tensor(f32(xsl, 512), f32(osl, 512),
                                        f32(xsl, 512), op=ALU.add),
                        reads=[(RF, osl, osl + 512), (RF, xsl, xsl + 512)],
                        writes=[(RF, xsl, xsl + 512)])
                    em('QG', lambda e, xsl=xsl, r0=r0, r1=r1, c0=c0,
                        g=g, hc=hc: e.dma_start(
                            out=out[r0:r1, c0:c0 + 512], in_=f32(xsl, 512)),
                        reads=[(RF, xsl, xsl + 512)],
                        writes=[('out', 4 * g + hc, 4 * g + hc + 1)],
                        dma=True, lane='bg')

        for (wsrc, wdst, wnm) in ((wv, wvb, 'wvb'), (wq, wqb, 'wqb'),
                                  (wk, wkb, 'wkb')):
            for r in range(4):
                d2d_cast_unit(wsrc, wdst, wnm, 512 * r, 512)()
        for t in range(16):
            p1_tile(t, t % 2)

        second0 = []
        for t in range(16):
            second0.append(lambda t=t: p1_tile(16 + t, t % 2))

        def mkslot(ms):
            def run():
                for m in ms:
                    m()
            return run
        for ms in slot_list:
            second0.append(mkslot(ms))

        class _Queue:

            def __init__(self, items):
                self.items = items
                self.i = 0

            def __call__(self):
                if self.i < len(self.items):
                    self.items[self.i]()
                    self.i += 1

        sq = _Queue(second0)
        p2_seq(0, [sq])
        while sq.i < min(16, len(sq.items)):
            sq()
        p2_seq(1, [sq])
        while sq.i < len(sq.items):
            sq()
        em('QS', lambda e: e.dma_start(out=f32(DSO, 2048), in_=ds),
           writes=[(RF, DSO, DSO + 2048)], dma=True)
        for w in range(NW):
            p3_window(w)

        f = S.finals()
        for cn, cnt in sorted(f.items()):
            if cnt > 0:
                streams['SYNC'].append(
                    lambda e, sem=sems[cn], cnt=cnt: e.wait_ge(sem, cnt))

        def _replay(stream):
            def body(eng):
                for fn in stream:
                    fn(eng)
            return body
        blk.tensor(_replay(streams['T']))
        blk.vector(_replay(streams['V']))
        blk.scalar(_replay(streams['S']))
        blk.sync(_replay(streams['SYNC']))
        blk.gpsimd(_replay(streams['GP']))

    return nc



def kernel(L0_attn_hidden_states, L0_attn_Wq, L0_attn_Wk, L0_attn_Wv,
           L0_attn_Wo, L0_ffn_gate_w, L0_ffn_up_w, L0_ffn_down_w,
           L0_ffn_gate_s, L0_ffn_up_s, L0_ffn_down_s):
    xs = np.ascontiguousarray(np.asarray(L0_attn_hidden_states,
                                         dtype=np.float32))
    base = {
        "wq": np.ascontiguousarray(L0_attn_Wq, dtype=np.float32),
        "wk": np.ascontiguousarray(L0_attn_Wk, dtype=np.float32),
        "wv": np.ascontiguousarray(L0_attn_Wv, dtype=np.float32),
        "wo": np.ascontiguousarray(L0_attn_Wo, dtype=np.float32),
        "gw": np.ascontiguousarray(L0_ffn_gate_w, dtype=np.float32),
        "uw": np.ascontiguousarray(L0_ffn_up_w, dtype=np.float32),
        "dwn": np.ascontiguousarray(L0_ffn_down_w, dtype=np.float32),
        "gs": np.ascontiguousarray(
            np.asarray(L0_ffn_gate_s, dtype=np.float32).reshape(44, 128).T),
        "us": np.ascontiguousarray(
            np.asarray(L0_ffn_up_s, dtype=np.float32).reshape(44, 128).T),
        "ds": np.ascontiguousarray(np.broadcast_to(
            np.asarray(L0_ffn_down_s, dtype=np.float32)[None, :],
            (128, DM))),
        "ident": np.eye(128, dtype=BFNP),
    }
    nc = build()
    in_maps = []
    for c in range(8):
        m = dict(base)
        m["x"] = np.ascontiguousarray(
            xs[BPC * c:BPC * (c + 1)].reshape(TOK, DM))
        in_maps.append(m)
    res = run_bass_kernel_spmd(nc, in_maps, core_ids=list(range(8)))
    shards = [r["out"].reshape(BPC, S_LEN, DM) for r in res.results]
    return np.ascontiguousarray(np.concatenate(shards, axis=0))
